# version 02 (final)
# baseline (speedup 1.0000x reference)
import math
import contextlib
import numpy as np
import ml_dtypes

import concourse.bass as bass
import concourse.mybir as mybir
from concourse.bass_utils import run_bass_kernel_spmd

F32 = mybir.dt.float32
BF16 = mybir.dt.bfloat16
U8 = mybir.dt.uint8
AL = mybir.AluOpType
AF = mybir.ActivationFunctionType
AX = mybir.AxisListType.X

S = 8192
D = 2048
NC = 8
OWN = S // NC
GUP = 16384
INV_SQRT_DH = 1.0 / math.sqrt(128.0)
LN_EPS = 1e-5
RMS_EPS = float(np.finfo(np.float32).eps)
KB = 1024

ENGINES = ("t", "v", "s", "g", "y")

PHASE_LIMIT = None
TAPS = ()
P1_TILES = 32


class Prog:
    def __init__(self):
        self.streams = {k: [] for k in ENGINES}


class Sem:
    def __init__(self, h):
        self.h = h
        self.n = 0


def build_program():
    nc = bass.Bass("TRN2", target_bir_lowering=False)

    x_full = nc.dram_tensor("x_full", [S, D], F32, kind="ExternalInput")
    x_own = nc.dram_tensor("x_own", [OWN, D], F32, kind="ExternalInput")
    w_qkvown = nc.dram_tensor("w_qkvown", [D, 768], F32, kind="ExternalInput")
    w_oown = nc.dram_tensor("w_oown", [256, D], F32, kind="ExternalInput")
    w_gu_shard = nc.dram_tensor("w_gu_shard", [GUP, 256], F32, kind="ExternalInput")
    w_p_shard = nc.dram_tensor("w_p_shard", [D, 1024], F32, kind="ExternalInput")
    ident_in = nc.dram_tensor("ident_bf", [128, 128], BF16, kind="ExternalInput")
    ones_in = nc.dram_tensor("ones_bf", [128, 128], BF16, kind="ExternalInput")
    out_own = nc.dram_tensor("out_own", [OWN, D], F32, kind="ExternalOutput")

    ag_gu_in = nc.dram_tensor("ag_gu_in", [2, 64, 128, 256], BF16)
    ag_gu_out = nc.dram_tensor("ag_gu_out", [16, 64, 128, 256], BF16)
    ag_p_in = nc.dram_tensor("ag_p_in", [8, 128, 2048], BF16)
    ag_p_out = nc.dram_tensor("ag_p_out", [64, 128, 2048], BF16)
    rs_in = nc.dram_tensor("rs_in", [S, D], F32)
    rs_out = nc.dram_tensor("rs_out", [OWN, D], F32)
    gth_scr = nc.dram_tensor("gth_scr", [1, 128], F32)

    prog = Prog()

    def W(eng, sem, val):
        prog.streams[eng].append(lambda e, s=sem, v=val: e.wait_ge(s.h, v))

    def EM(eng, fn, inc=None, by=1):
        if inc is None:
            prog.streams[eng].append(fn)
        else:
            inc.n += by
            prog.streams[eng].append(lambda e, f=fn, s=inc, b=by: f(e).then_inc(s.h, b))

    def DR(eng):
        prog.streams[eng].append(lambda e: e.drain())

    def EMD(eng, fn, inc, by=1):
        prog.streams[eng].append(fn)
        inc.n += by
        prog.streams[eng].append(lambda e, s=inc, b=by: e.drain().then_inc(s.h, b))

    with contextlib.ExitStack() as es:
        POOL_B = 207 * KB
        pool = es.enter_context(nc.sbuf_tensor("pool", [128, POOL_B], U8))
        PS = [es.enter_context(nc.psum_tensor(f"ps{i}", [128, 2048], F32)) for i in range(2)]

        def sbuf(off_b, n_elem, dt):
            nbytes = n_elem * mybir.dt.size(dt)
            assert off_b % 32 == 0 and off_b + nbytes <= POOL_B, (off_b, nbytes)
            return pool[:, off_b:off_b + nbytes].bitcast(dt)

        def sem(name):
            return Sem(es.enter_context(nc.semaphore(name)))

        O_CONST = 0
        O_R1 = 1 * KB
        O_Q = O_R1
        O_K = O_R1 + 32 * KB
        O_V = O_R1 + 64 * KB
        O_X = O_R1 + 96 * KB
        O_R2 = 129 * KB
        O_R3 = 161 * KB
        O_R4 = 194 * KB

        ident_bf = sbuf(O_CONST, 128, BF16)
        ones_bf = sbuf(O_CONST + 256, 128, BF16)

        xbuf = [sbuf(O_X + i * 16 * KB, 4096, F32) for i in range(2)]
        wqkv_sb = sbuf(O_R2, 12288, BF16)
        trash = sbuf(O_R2 + 24 * KB, 2048, F32)
        xn_tm = [sbuf(O_R3 + i * 4 * KB, 2048, BF16) for i in range(4)]
        xn_fm = [sbuf(O_R3 + 16 * KB + i * 8 * KB, 4096, BF16) for i in range(2)]
        qbuf = sbuf(O_Q, 16384, BF16)
        kbuf = sbuf(O_K, 16384, BF16)
        vbuf = sbuf(O_V, 16384, BF16)

        accb = sbuf(O_X, 8192, F32)
        Ebuf = [sbuf(O_R2 + i * 16 * KB, 8192, BF16) for i in range(2)]
        q2buf = Ebuf[0]
        accbf = Ebuf[0]
        wgbuf = [sbuf(O_R3 + i * 2 * KB, 1024, BF16) for i in range(2)]
        tbuf = sbuf(O_R3 + 4 * KB, 1024, BF16)
        whatb = [sbuf(O_R3 + 6 * KB + i * 2 * KB, 1024, BF16) for i in range(2)]
        qab = sbuf(O_R3 + 10 * KB, 1024, BF16)
        qbb = sbuf(O_R3 + 12 * KB, 1024, BF16)
        gsum = sbuf(O_R3 + 14 * KB, 16, F32)
        scale_bf = sbuf(O_R3 + 14 * KB + 256, 16, BF16)
        recipb = sbuf(O_R3 + 14 * KB + 512, 16, F32)
        tblk = [sbuf(O_R3 + 15 * KB + i * 256, 128, BF16) for i in range(8)]

        cst = [sbuf(O_X + i * 8 * KB, 2048, F32) for i in range(4)]
        w_oT = [sbuf(O_Q + i * 4 * KB, 2048, BF16) for i in range(2)]

        hbuf = sbuf(O_R1, 65536, BF16)
        xbuf2 = [sbuf(O_R2 + i * 16 * KB, 4096, F32) for i in range(2)]
        trash2 = sbuf(O_R4 + 3 * KB, 2048, F32)
        xn_tm2 = [sbuf(O_X + i * 4 * KB, 2048, BF16) for i in range(2)]
        xn_own = sbuf(O_R3, 16384, BF16)
        wslab = [sbuf(O_R2 + i * 16 * KB, 8192, BF16) for i in range(2)]
        sgt = [sbuf(O_R4 + 3 * KB + i * 4 * KB, 1024, F32) for i in range(2)]
        h2b = [sbuf(O_R2 + 24 * KB + i * 2 * KB, 1024, BF16) for i in range(2)]
        slab = [sbuf(O_R2 + i * 4 * KB, 2048, BF16) for i in range(4)]
        gtmp = [sbuf(O_R2 + 16 * KB + i * 8 * KB, 2048, F32) for i in range(2)]
        s_chunk = [sbuf(O_R3 + i * 8 * KB, 2048, F32) for i in range(2)]
        xtm = [sbuf(O_R3 + 16 * KB + i * 8 * KB, 2048, F32) for i in range(2)]
        outsb = sbuf(O_R3 + 16 * KB, 2048, F32)
        outsb = sbuf(O_R4 + 3 * KB, 2048, F32)

        bias_all = [sbuf(O_R4 + i * 256, 64, F32) for i in range(2)]
        den = [sbuf(O_R4 + 512 + i * 32, 4, F32) for i in range(2)]
        wt = [sbuf(O_R4 + 576 + i * 32, 1, F32) for i in range(2)]
        sumsq_ln = [sbuf(O_R4 + 640 + 32 * i, 1, F32) for i in range(2)]
        negsum = [sbuf(O_R4 + 704 + 32 * i, 1, F32) for i in range(2)]
        nmb = [sbuf(O_R4 + 768 + 32 * i, 1, F32) for i in range(2)]
        v1b = [sbuf(O_R4 + 832 + 32 * i, 1, F32) for i in range(2)]
        m2b = [sbuf(O_R4 + 896 + 32 * i, 1, F32) for i in range(2)]
        m2nb_ = [sbuf(O_R4 + 960 + 32 * i, 1, F32) for i in range(2)]
        varb = [sbuf(O_R4 + 12 * KB + 32 * i, 1, F32) for i in range(2)]
        stdb = [sbuf(O_R4 + 12 * KB + 64 + 32 * i, 1, F32) for i in range(2)]
        rstdb = [sbuf(O_R4 + 12 * KB + 128 + 32 * i, 1, F32) for i in range(2)]
        nq2 = sbuf(O_R4 + 1 * KB, 64, F32)
        nk2 = sbuf(O_R4 + 1 * KB + 256, 64, F32)
        nkmax = sbuf(O_R4 + 1 * KB + 512, 1, F32)
        nkrow = sbuf(O_R4 + 1 * KB + 544, 128, F32)
        mx2 = sbuf(O_R4 + 2 * KB, 1, F32)
        mxk = sbuf(O_R4 + 2 * KB + 32, 1, F32)
        X_bf = sbuf(O_R4 + 2 * KB + 64, 8, BF16)
        maxkb = sbuf(O_R4 + 2 * KB + 96, 1, F32)
        sqq = sbuf(O_R4 + 2 * KB + 128, 64, F32)
        inv_bc = sbuf(O_R4 + 3 * KB, 1024, F32)
        itmp = sbuf(O_R4 + 7 * KB, 1024, F32)
        epsb = sbuf(O_R4 + 11 * KB, 1, F32)

        s_ld = sem("s_ld"); s_ldg = sem("s_ldg")
        s_ldc = sem("s_ldc"); s_wol = sem("s_wol"); s_x2 = sem("s_x2")
        s_lna = sem("s_lna")
        s_xld = [sem("s_xld0"), sem("s_xld1")]
        s_qld = [sem("s_qld0"), sem("s_qld1")]
        s_wq = [sem("s_wq0"), sem("s_wq1")]
        s_qst = [sem("s_qst0"), sem("s_qst1")]
        s_cstp = [sem(f"s_cstp{i}") for i in range(4)]
        s_w3p = [sem("s_w3p0"), sem("s_w3p1")]
        s_slp = [sem(f"s_slp{i}") for i in range(4)]
        s_sq = sem("s_sq"); s_var = sem("s_var"); s_std = sem("s_std"); s_ln = sem("s_ln")
        s_tp = sem("s_tp"); s_fm = sem("s_fm"); s_qkv = sem("s_qkv"); s_qkve = sem("s_qkve")
        s_np = sem("s_np"); s_npd = sem("s_npd"); s_bias = sem("s_bias")
        s_sc = [sem("s_sc0"), sem("s_sc1")]
        s_exp = [sem("s_exp0"), sem("s_exp1")]
        s_den = sem("s_den"); s_acc = sem("s_acc"); s_accg = sem("s_accg")
        s_qv = sem("s_qv"); s_qw = sem("s_qw"); s_ag = sem("s_ag")
        s_bv = sem("s_bv"); s_bt = sem("s_bt")
        s_wo = sem("s_wo"); s_woe = sem("s_woe"); s_cst = sem("s_cst"); s_rs = sem("s_rs")
        s_w3 = sem("s_w3"); s_gu = sem("s_gu"); s_sil = sem("s_sil"); s_h = sem("s_h")
        s_ss = sem("s_ss"); s_ssm = sem("s_ssm"); s_hn = sem("s_hn"); s_hng = sem("s_hng")
        s_sl = sem("s_sl"); s_sl2 = sem("s_sl2"); s_sx = sem("s_sx")
        s_p5 = sem("s_p5"); s_p5s = sem("s_p5s"); s_p5e = sem("s_p5e"); s_out = sem("s_out")

        EM("y", lambda e: e.dma_start(out=ident_bf, in_=ident_in[:, :]), s_ldc, 16)
        EM("y", lambda e: e.dma_start(out=ones_bf, in_=ones_in[:, :]), s_ldc, 16)
        for k in range(16):
            EM("g", lambda e, k=k: e.dma_start(
                out=wqkv_sb[:, k * 768:(k + 1) * 768],
                in_=w_qkvown[128 * k:128 * (k + 1), :]), s_ldg, 16)
        s_ini = sem("s_ini")
        EM("v", lambda e: e.memset(X_bf, 0.0))
        EMD("v", lambda e: e.memset(epsb, LN_EPS), s_ini)
        LDG_BASE = s_ldg.n

        tp_ctr = [0]
        sub_ctr = [0]
        sub_tp_done = {}

        def ln_tile(src, T, xb_list, xnt_list, trash_buf, evac_dst_fn, qkv,
                    ln_done, pend):
            nb = len(xb_list)
            par = T % nb
            xcur = xb_list[par]
            xsem = s_xld[par] if nb > 1 else s_x2
            if T - nb in ln_done:
                W("y", s_ln, ln_done[T - nb])
            EM("y", lambda e, T=T, xcur=xcur, src=src: e.dma_start(
                out=xcur.rearrange("p (a d) -> p a d", a=2),
                in_=src[256 * T:256 * (T + 1), :].rearrange("(a p) d -> p a d", p=128)),
               xsem, 16)
            ld_target = xsem.n

            def front(n_sub, a, xs):
                pr = n_sub % 2
                W("v", xsem, ld_target)
                EMD("v", lambda e, xs=xs: e.reduce_sum(negsum[pr], xs, axis=AX,
                                                       negate=True), s_var)
                W("s", xsem, ld_target)
                if n_sub == 0:
                    W("s", s_ini, 1)
                W("s", s_var, s_var.n)
                EM("s", lambda e, xs=xs, pr=pr: e.activation(
                    trash_buf, xs, AF.Square, accum_out=sumsq_ln[pr]))
                DR("s")
                EM("s", lambda e, pr=pr: e.mul(nmb[pr], negsum[pr], 1.0 / 2048.0))
                EM("s", lambda e, pr=pr: e.mul(v1b[pr], sumsq_ln[pr], 1.0 / 2048.0))
                DR("s")
                EM("s", lambda e, pr=pr: e.activation(m2b[pr], nmb[pr], AF.Square))
                DR("s")
                EM("s", lambda e, pr=pr: e.activation(m2nb_[pr], m2b[pr], AF.Identity,
                                                      bias=epsb[:, 0:1], scale=-1.0))
                DR("s")
                EM("s", lambda e, pr=pr: e.add(varb[pr], v1b[pr], m2nb_[pr][:, 0:1]))
                DR("s")
                EM("s", lambda e, pr=pr: e.activation(stdb[pr], varb[pr], AF.Sqrt), s_std)

            def back(state):
                n_sub, a, xs, T_, par_, _std_n = state
                pr = n_sub % 2
                ntm = len(xnt_list)
                xnt = xnt_list[n_sub % ntm]
                W("v", s_std, _std_n)
                EM("v", lambda e, pr=pr: e.reciprocal(rstdb[pr], stdb[pr]))
                DR("v")
                if n_sub - ntm in sub_tp_done:
                    W("v", s_tp, sub_tp_done[n_sub - ntm])
                EMD("v", lambda e, xs=xs, xnt=xnt, pr=pr: e.tensor_scalar(
                    xnt, xs, nmb[pr], rstdb[pr], op0=AL.add, op1=AL.mult), s_ln)
                if T_ == 0 and a == 0:
                    W("t", s_ldc, 32)
                W("t", s_ln, s_ln.n)
                for b in range(4):
                    bidx = tp_ctr[0]
                    tp_ctr[0] += 1
                    pp = bidx % 2
                    if bidx >= 2:
                        W("t", s_fm, bidx - 1)
                    for cc in range(4 * b, 4 * b + 4):
                        EM("t", lambda e, pp=pp, cc=cc, xnt=xnt: e.matmul(
                            PS[pp][:, 1536 + (cc % 4) * 128:1536 + (cc % 4 + 1) * 128],
                            xnt[:, cc * 128:(cc + 1) * 128], ident_bf,
                            start=True, stop=True),
                           s_tp if cc == 4 * b + 3 else None)
                    W("v", s_tp, s_tp.n)
                    if qkv and T_ >= 2 and b == 0 and a == 0:
                        W("v", s_qkv, T_ - 1)
                    dst = evac_dst_fn(T_, b, a)
                    EMD("v", lambda e, pp=pp, dst=dst: e.tensor_copy(
                        dst, PS[pp][:, 1536:2048].rearrange("p (c m) -> p c m", c=4)),
                        s_fm)
                sub_tp_done[n_sub] = s_tp.n
                if qkv and a == 1:
                    emit_qkv(T_, par_)
                if a == 1:
                    ln_done[T_] = s_ln.n

            def emit_qkv(T_, par_):
                W("t", s_fm, s_fm.n)
                if T_ == 0:
                    W("t", s_ldg, 256)
                if T_ >= 2:
                    W("t", s_qkve, T_ - 1)
                parq = T_ % 2
                for o in range(6):
                    for k in range(16):
                        EM("t", lambda e, o=o, k=k, parq=parq: e.matmul(
                            PS[parq][:, o * 256:(o + 1) * 256],
                            wqkv_sb[:, (k * 6 + o) * 128:(k * 6 + o + 1) * 128],
                            xn_fm[parq][:, k * 256:(k + 1) * 256],
                            start=(k == 0), stop=(k == 15)),
                           s_qkv if (o == 5 and k == 15) else None)
                W("v", s_qkv, s_qkv.n)
                dsts = [qbuf[:, 256 * T_:256 * (T_ + 1)],
                        qbuf[:, 8192 + 256 * T_:8192 + 256 * (T_ + 1)],
                        kbuf[:, 256 * T_:256 * (T_ + 1)],
                        kbuf[:, 8192 + 256 * T_:8192 + 256 * (T_ + 1)],
                        vbuf[:, 256 * T_:256 * (T_ + 1)],
                        vbuf[:, 8192 + 256 * T_:8192 + 256 * (T_ + 1)]]
                for o in range(6):
                    if o < 5:
                        EM("v", lambda e, o=o, d=dsts[o], parq=parq: e.tensor_copy(
                            d, PS[parq][:, o * 256:(o + 1) * 256]))
                    else:
                        EMD("v", lambda e, o=o, d=dsts[o], parq=parq: e.tensor_copy(
                            d, PS[parq][:, o * 256:(o + 1) * 256]), s_qkve)

            for a in range(2):
                n_sub = sub_ctr[0]
                sub_ctr[0] += 1
                xs = xcur[:, a * 2048:(a + 1) * 2048]
                front(n_sub, a, xs)
                st = (n_sub, a, xs, T, par, s_std.n)
                if pend:
                    back(pend.pop(0))
                pend.append(st)
            return back

        def p1():
            def evac_dst(T, b, a):
                par = T % 2
                v = xn_fm[par].rearrange("p (c m) -> p c m", m=256)
                return v[:, 4 * b:4 * b + 4, a * 128:(a + 1) * 128]
            ln_done, pend = {}, []
            for T in range(P1_TILES):
                bk = ln_tile(x_full, T, xbuf, xn_tm, trash, evac_dst, True,
                             ln_done, pend)
            bk(pend.pop(0))

        def p2_prep():
            W("s", s_qkve, s_qkve.n)
            W("t", s_qkve, s_qkve.n)
            for hl in range(2):
                for mi, (mat, dstn) in enumerate(((qbuf, nq2), (kbuf, nk2))):
                    if hl + mi > 0:
                        W("s", s_tp, tp_ctr[0])
                    EM("s", lambda e, mat=mat, hl=hl: e.activation(
                        q2buf, mat[:, hl * 8192:(hl + 1) * 8192], AF.Square), s_np)
                    W("t", s_np, s_np.n)
                    for b in range(16):
                        bidx = tp_ctr[0]
                        tp_ctr[0] += 1
                        pp = bidx % 2
                        if bidx >= 2:
                            W("t", s_fm, bidx - 1)
                        for cc in range(4 * b, 4 * b + 4):
                            EM("t", lambda e, pp=pp, cc=cc: e.matmul(
                                PS[pp][:, 1536 + (cc % 4) * 128:1536 + (cc % 4 + 1) * 128],
                                q2buf[:, cc * 128:(cc + 1) * 128], ident_bf,
                                start=True, stop=True),
                               s_tp if cc == 4 * b + 3 else None)
                        W("v", s_tp, s_tp.n)
                        EMD("v", lambda e, pp=pp, b=b, dstn=dstn: e.reduce_sum(
                            dstn[:, 4 * b:4 * b + 4],
                            PS[pp][:, 1536:2048].rearrange("p (c m) -> p c m", c=4),
                            axis=AX), s_fm)
                EMD("v", lambda e: e.reduce_max(nkmax, nk2, axis=AX), s_np)
                W("y", s_np, s_np.n)
                EM("y", lambda e: e.dma_start(out=gth_scr[:, :], in_=nkmax,
                                              allow_slow_non_contiguous=True), s_npd, 16)
                W("y", s_npd, s_npd.n)
                EM("y", lambda e: e.dma_start(out=nkrow[0:1, :], in_=gth_scr[:, :]), s_npd, 16)
                W("v", s_npd, s_npd.n)
                EMD("v", lambda e: e.reduce_max(mx2[0:1, :], nkrow[0:1, :], axis=AX), s_np)
                W("s", s_np, s_np.n)
                EM("s", lambda e: e.activation(mxk[0:1, :], mx2[0:1, :], AF.Sqrt), s_np)
                W("v", s_np, s_np.n)
                EMD("v", lambda e: e.tensor_copy(X_bf[0:1, 0:1], mxk[0:1, :]), s_np)
                W("t", s_np, s_np.n)
                EM("t", lambda e: e.matmul(PS[0][:, 0:8], ones_bf, X_bf,
                                           start=True, stop=True), s_np)
                W("v", s_np, s_np.n)
                EMD("v", lambda e: e.tensor_copy(maxkb, PS[0][:, 0:1]), s_np)
                W("s", s_np, s_np.n)
                W("s", s_fm, s_fm.n)
                EM("s", lambda e: e.activation(sqq, nq2, AF.Sqrt), s_np)
                W("v", s_np, s_np.n)
                EMD("v", lambda e, hl=hl: e.tensor_scalar(
                    bias_all[hl], sqq, maxkb, -INV_SQRT_DH, op0=AL.mult, op1=AL.mult),
                    s_bias)

        N_QCH = 48

        def quant_load(c):
            if c < 32:
                src = w_gu_shard[512 * c:512 * (c + 1), :].rearrange("(a p) d -> p a d", p=128)
                dst = wgbuf[c % 2].rearrange("p (a d) -> p a d", a=4)
            else:
                p = c - 32
                src = w_p_shard[128 * p:128 * (p + 1), :]
                dst = wgbuf[c % 2]
            EM("g", lambda e, src=src, dst=dst: e.dma_start(out=dst, in_=src),
               s_qld[c % 2], 16)

        wq_after_chunk = {}
        qst_after_chunk = {}

        def quant_vec(c):
            W("v", s_qld[c % 2], 16 * (c // 2 + 1))
            if c - 2 in wq_after_chunk:
                W("v", s_wq[c % 2], wq_after_chunk[c - 2])
            wv = wgbuf[c % 2].rearrange("p (g k) -> p g k", k=64)
            tv = tbuf.rearrange("p (g k) -> p g k", k=64)
            qav = qab.rearrange("p (g k) -> p g k", k=64)
            qbv = qbb.rearrange("p (g k) -> p g k", k=64)
            whv = whatb[c % 2].rearrange("p (g k) -> p g k", k=64)
            scv = scale_bf.unsqueeze(2).broadcast_to([128, 16, 64])
            rcv = recipb.unsqueeze(2).broadcast_to([128, 16, 64])
            EM("v", lambda e, wv=wv: e.reduce_sum(gsum, wv, axis=AX, apply_absolute_value=True))
            DR("v")
            EM("v", lambda e: e.tensor_scalar(scale_bf, gsum, 1.0 / 64.0, 1e-8,
                                              op0=AL.mult, op1=AL.max))
            DR("v")
            EM("v", lambda e: e.reciprocal(recipb, scale_bf))
            DR("v")
            EM("v", lambda e, wv=wv, rcv=rcv, tv=tv: e.tensor_tensor(tv, wv, rcv, op=AL.mult))
            DR("v")
            EM("v", lambda e, tv=tv, qav=qav: e.tensor_scalar(qav, tv, 0.5, None, op0=AL.is_gt))
            EM("v", lambda e, tv=tv, qbv=qbv: e.tensor_scalar(qbv, tv, -0.5, None, op0=AL.is_lt))
            DR("v")
            EM("v", lambda e, qav=qav, qbv=qbv: e.tensor_sub(qav, qav, qbv))
            DR("v")
            EMD("v", lambda e, qav=qav, scv=scv, whv=whv: e.tensor_tensor(
                whv, qav, scv, op=AL.mult), s_qv)

        def quant_dma(c):
            W("y", s_qv, c + 1)
            if c - 1 in qst_after_chunk:
                W("y", s_qst[(c - 1) % 2], qst_after_chunk[c - 1])
            if c - 2 in qst_after_chunk:
                W("y", s_qst[c % 2], qst_after_chunk[c - 2])
            if c < 32:
                srcs = [(whatb[c % 2][:, a * 256 + il * 128:a * 256 + (il + 1) * 128],
                         ag_gu_in[il, (4 * c + a) // 2, :,
                                  ((4 * c + a) % 2) * 128:((4 * c + a) % 2 + 1) * 128])
                        for a in range(4) for il in range(2)]
            else:
                p = c - 32
                srcs = [(whatb[c % 2][:, il * 128:(il + 1) * 128],
                         ag_p_in[il, :, p * 128:(p + 1) * 128]) for il in range(8)]
            for bi, (sap, _) in enumerate(srcs):
                EM("y", lambda e, sap=sap, bi=bi: e.dma_start(
                    out=tblk[bi], in_=sap, transpose=True), s_wq[c % 2], 16)
            wq_after_chunk[c] = s_wq[c % 2].n
            W("y", s_wq[c % 2], s_wq[c % 2].n)
            for bi, (_, dap) in enumerate(srcs):
                EM("y", lambda e, dap=dap, bi=bi: e.dma_start(
                    out=dap, in_=tblk[bi]), s_qst[c % 2], 16)
            qst_after_chunk[c] = s_qst[c % 2].n

        grp_cnt = [0, 0]

        def p2_head(hl):
            qh_off = hl * 8192
            for t in range(64):
                gt = 64 * hl + t
                par2 = gt % 2
                if t == 0:
                    W("t", s_bias, 2)
                    W("t", s_bv, s_bv.n)
                sc_target = [0, 0, 0, 0]
                for g in range(4):
                    pp = (4 * t + g) % 2
                    grp_cnt[pp] += 1
                    if grp_cnt[pp] > 1:
                        W("t", s_exp[pp], grp_cnt[pp] - 1)
                    for sg in range(4):
                        seg = 4 * g + sg
                        EM("t", lambda e, pp=pp, sg=sg, seg=seg, t=t, qh_off=qh_off: e.matmul(
                            PS[pp][:, sg * 512:(sg + 1) * 512],
                            qbuf[:, qh_off + t * 128:qh_off + (t + 1) * 128],
                            kbuf[:, qh_off + seg * 512:qh_off + (seg + 1) * 512],
                            start=True, stop=True),
                           s_sc[pp] if sg == 3 else None)
                    sc_target[g] = s_sc[pp].n
                if gt == 0:
                    W("s", s_bias, 2)
                for g in range(4):
                    pp = (4 * t + g) % 2
                    W("s", s_sc[pp], sc_target[g])
                    if g == 0:
                        if gt >= 2:
                            W("s", s_acc, gt - 1)
                            W("s", s_accg, gt - 1)
                            W("s", s_den, gt - 1)
                        if hl == 1 and t == 0:
                            W("s", s_bt, 16)
                    EM("s", lambda e, pp=pp, g=g, par2=par2, t=t, hl=hl: e.activation(
                        Ebuf[par2][:, g * 2048:(g + 1) * 2048], PS[pp][:, :], AF.Exp,
                        bias=bias_all[hl][:, t:t + 1], scale=INV_SQRT_DH,
                        accum_out=den[par2][:, g:g + 1]), s_exp[pp])
                W("v", s_exp[0], s_exp[0].n)
                W("v", s_exp[1], s_exp[1].n)
                if gt >= 2:
                    W("v", s_accg, gt - 1)
                EM("v", lambda e, par2=par2: e.reduce_sum(wt[par2], den[par2], axis=AX))
                DR("v")
                EMD("v", lambda e, par2=par2: e.reciprocal(wt[par2], wt[par2]), s_den)
                if t == 0:
                    EMD("v", lambda e, par2=par2: e.tensor_scalar(
                        accb[:, 0:8192], Ebuf[par2][:, 0:8192], wt[par2], None,
                        op0=AL.mult), s_acc)
                else:
                    EMD("v", lambda e, par2=par2: e.scalar_tensor_tensor(
                        accb[:, 0:8192], Ebuf[par2][:, 0:8192], wt[par2], accb[:, 0:8192],
                        op0=AL.mult, op1=AL.add), s_acc)
                W("g", s_den, s_den.n)
                EM("g", lambda e: e.engine_nop(), s_accg)
                qc = gt
                if qc == 0:
                    quant_load(0)
                    quant_load(1)
                if qc < N_QCH:
                    quant_vec(qc)
                    quant_dma(qc)
                    if qc + 2 < N_QCH:
                        W("g", s_qv, qc + 1)
                        quant_load(qc + 2)
                if qc == N_QCH - 1:
                    W("g", s_qst[0], s_qst[0].n)
                    W("g", s_qst[1], s_qst[1].n)
                    EM("g", lambda e: e.collective_compute(
                        "AllGather", AL.bypass, replica_groups=[list(range(NC))],
                        ins=[ag_gu_in.ap().opt()], outs=[ag_gu_out.ap().opt()]), s_ag)
                    EM("g", lambda e: e.collective_compute(
                        "AllGather", AL.bypass, replica_groups=[list(range(NC))],
                        ins=[ag_p_in.ap().opt()], outs=[ag_p_out.ap().opt()]), s_ag)

            W("v", s_acc, 64 * (hl + 1))
            W("v", s_accg, 64 * (hl + 1))
            EMD("v", lambda e: e.tensor_copy(accbf, accb), s_bv)
            base_bv = s_bv.n
            W("t", s_bv, base_bv)
            for seg in range(16):
                slot = seg % 4
                pp, half = slot % 2, slot // 2
                if seg >= 4:
                    W("t", s_bv, base_bv + (seg - 3))
                EM("t", lambda e, pp=pp, half=half, seg=seg, hl=hl: e.matmul(
                    PS[pp][:, half * 512:(half + 1) * 512], ones_bf,
                    accbf[:, seg * 512:(seg + 1) * 512], start=True, stop=True), s_bt)
                W("v", s_bt, s_bt.n)
                EMD("v", lambda e, pp=pp, half=half, seg=seg, hl=hl: e.tensor_tensor(
                    vbuf[:, hl * 8192 + seg * 512:hl * 8192 + (seg + 1) * 512],
                    vbuf[:, hl * 8192 + seg * 512:hl * 8192 + (seg + 1) * 512],
                    PS[pp][:, half * 512:(half + 1) * 512], op=AL.mult), s_bv)

        def p4():
            W("g", s_sc[0], s_sc[0].n)
            W("g", s_sc[1], s_sc[1].n)
            for kc in range(2):
                EM("g", lambda e, kc=kc: e.dma_start(
                    out=w_oT[kc], in_=w_oown[128 * kc:128 * (kc + 1), :]), s_wol, 16)
            W("t", s_bv, s_bv.n)
            W("t", s_wol, 32)
            W("t", s_fm, s_fm.n)
            for tb in range(64):
                pp = tb % 2
                slot = tb % 4
                if tb >= 2:
                    W("t", s_woe, tb - 1)
                for kc in range(2):
                    for ds in range(4):
                        EM("t", lambda e, pp=pp, tb=tb, kc=kc, ds=ds: e.matmul(
                            PS[pp][:, ds * 512:(ds + 1) * 512],
                            vbuf[:, kc * 8192 + tb * 128:kc * 8192 + (tb + 1) * 128],
                            w_oT[kc][:, ds * 512:(ds + 1) * 512],
                            start=(kc == 0), stop=(kc == 1)),
                           s_wo if (kc == 1 and ds == 3) else None)
                W("s", s_wo, s_wo.n)
                if tb >= 4:
                    W("s", s_cstp[slot], s_cstp[slot].n)
                EMD("s", lambda e, slot=slot, pp=pp: e.copy(cst[slot], PS[pp][:, :]),
                    s_woe)
                W("y", s_woe, s_woe.n)
                EM("y", lambda e, slot=slot, tb=tb: e.dma_start(
                    out=rs_in[tb * 128:(tb + 1) * 128, :], in_=cst[slot]),
                   s_cstp[slot], 16)
            for sl_ in range(4):
                W("g", s_cstp[sl_], s_cstp[sl_].n)
            EM("g", lambda e: e.collective_compute(
                "ReduceScatter", AL.add, replica_groups=[list(range(NC))],
                ins=[rs_in.ap().opt()], outs=[rs_out.ap().opt()]), s_rs)

        def p25():
            xnov = xn_own.rearrange("p (c m) -> p c m", m=1024)

            def evac_dst(T, b, a):
                return xnov[:, 4 * b:4 * b + 4, T * 256 + a * 128:T * 256 + (a + 1) * 128]
            for pi_ in range(2):
                W("y", s_qst[pi_], s_qst[pi_].n)
                W("v", s_qst[pi_], s_qst[pi_].n)
            W("y", s_bt, s_bt.n)
            W("y", s_acc, s_acc.n)
            ln_done, pend = {}, []
            for T in range(4):
                bk = ln_tile(x_own, T, xbuf2, xn_tm2, trash2, evac_dst, False,
                             ln_done, pend)
            bk(pend.pop(0))

        def p3():
            W("y", s_ag, 2)
            W("y", s_ln, s_ln.n)
            W("y", s_fm, s_fm.n)
            W("t", s_ag, 2)
            W("t", s_fm, s_fm.n)
            for sl_ in range(4):
                W("v", s_cstp[sl_], s_cstp[sl_].n)
            for q in range(32):
                buf = q % 2
                if q >= 2:
                    W("y", s_gu, 2 * (q - 1))
                for half, jg in ((0, q), (1, 32 + q)):
                    EM("y", lambda e, buf=buf, half=half, jg=jg: e.dma_start(
                        out=wslab[buf][:, half * 4096:(half + 1) * 4096]
                            .rearrange("p (i m) -> p i m", i=16),
                        in_=ag_gu_out[:, jg, :, :].rearrange("i p m -> p i m")),
                       s_w3p[buf], 16)
                W("t", s_w3p[buf], s_w3p[buf].n)
                for pi in range(2):
                    g = 2 * q + pi
                    pp = g % 2
                    if g >= 2:
                        W("t", s_h, g - 1)
                    for half in range(2):
                        for i in range(16):
                            for hh in range(2):
                                EM("t", lambda e, pp=pp, half=half, i=i, hh=hh, pi=pi, buf=buf: e.matmul(
                                    PS[pp][:, half * 1024 + hh * 512:half * 1024 + (hh + 1) * 512],
                                    wslab[buf][:, half * 4096 + i * 256 + pi * 128:
                                               half * 4096 + i * 256 + (pi + 1) * 128],
                                    xn_own[:, i * 1024 + hh * 512:i * 1024 + (hh + 1) * 512],
                                    start=(i == 0), stop=(i == 15)),
                                   s_gu if (half == 1 and i == 15 and hh == 1) else None)
                    W("s", s_gu, s_gu.n)
                    EM("s", lambda e, pp=pp: e.activation(sgt[pp], PS[pp][:, 0:1024], AF.Silu),
                       s_sil)
                    W("v", s_sil, s_sil.n)
                    EMD("v", lambda e, pp=pp, g=g: e.tensor_tensor(
                        hbuf[:, g * 1024:(g + 1) * 1024], sgt[pp], PS[pp][:, 1024:2048],
                        op=AL.mult), s_h)

        def p3b():
            W("s", s_h, s_h.n)
            W("t", s_h, s_h.n)
            for g in range(64):
                pp = g % 2
                if g >= 2:
                    W("s", s_ssm, g - 1)
                EM("s", lambda e, g=g, pp=pp: e.activation(
                    h2b[pp], hbuf[:, g * 1024:(g + 1) * 1024], AF.Square), s_ss)
                W("t", s_ss, s_ss.n)
                for half in range(2):
                    EM("t", lambda e, pp=pp, half=half, g=g: e.matmul(
                        PS[0][:, half * 512:(half + 1) * 512], ones_bf,
                        h2b[pp][:, half * 512:(half + 1) * 512],
                        start=(g == 0), stop=(g == 63)),
                       s_ssm if half == 1 else None)
            W("v", s_ssm, s_ssm.n)
            EMD("v", lambda e: e.tensor_scalar(itmp, PS[0][:, 0:1024], 1.0 / 8192.0, RMS_EPS,
                                               op0=AL.mult, op1=AL.add), s_np)
            W("s", s_np, s_np.n)
            EM("s", lambda e: e.activation(itmp, itmp, AF.Sqrt), s_np)
            W("v", s_np, s_np.n)
            EMD("v", lambda e: e.reciprocal(inv_bc, itmp), s_np)
            W("v", s_np, s_np.n)
            W("g", s_np, s_np.n)
            for g in range(64):
                eng = "v" if g % 2 == 0 else "g"
                if g < 62:
                    EM(eng, lambda e, g=g: e.tensor_tensor(
                        hbuf[:, g * 1024:(g + 1) * 1024], hbuf[:, g * 1024:(g + 1) * 1024],
                        inv_bc, op=AL.mult))
                else:
                    EMD(eng, lambda e, g=g: e.tensor_tensor(
                        hbuf[:, g * 1024:(g + 1) * 1024], hbuf[:, g * 1024:(g + 1) * 1024],
                        inv_bc, op=AL.mult), s_hn if eng == "v" else s_hng)

        def p5():
            W("t", s_hn, s_hn.n)
            W("t", s_hng, s_hng.n)
            W("t", s_np, s_np.n)
            W("y", s_gu, s_gu.n)
            sl_ctr = [0]
            for ppair in range(4):
                tbs = (2 * ppair, 2 * ppair + 1)
                W("y", s_rs, 1)
                if ppair >= 1:
                    W("y", s_p5e, 2 * ppair)
                for tb in tbs:
                    EM("y", lambda e, tb=tb: e.dma_start(
                        out=s_chunk[tb % 2], in_=rs_out[tb * 128:(tb + 1) * 128, :]), s_sx, 16)
                    EM("y", lambda e, tb=tb: e.dma_start(
                        out=xtm[tb % 2], in_=x_own[tb * 128:(tb + 1) * 128, :]), s_sx, 16)
                if ppair >= 1:
                    W("t", s_p5s, 2 * ppair)
                for i in range(64):
                    sl = sl_ctr[0]
                    sl_ctr[0] += 1
                    if sl >= 4:
                        W("y", s_sl2, sl - 3)
                    EM("y", lambda e, i=i, sl=sl: e.dma_start(
                        out=slab[sl % 4], in_=ag_p_out[i, :, :]), s_slp[sl % 4], 16)
                    W("t", s_slp[sl % 4], s_slp[sl % 4].n)
                    for ti, tb in enumerate(tbs):
                        pp = tb % 2
                        for ds in range(4):
                            EM("t", lambda e, pp=pp, i=i, tb=tb, ds=ds, sl=sl: e.matmul(
                                PS[pp][:, ds * 512:(ds + 1) * 512],
                                hbuf[:, i * 1024 + tb * 128:i * 1024 + (tb + 1) * 128],
                                slab[sl % 4][:, ds * 512:(ds + 1) * 512],
                                start=(i == 0), stop=(i == 63)),
                               s_sl2 if (ti == 1 and ds == 3) else None,
                               )
                for tb in tbs:
                    pp = tb % 2
                    W("s", s_sl2, 64 * (ppair + 1))
                    EM("s", lambda e, pp=pp: e.activation(gtmp[pp], PS[pp][:, :], AF.Silu),
                       s_p5s)
                    W("v", s_p5s, s_p5s.n)
                    W("v", s_sx, s_sx.n)
                    W("v", s_out, 16 * tb)
                    EM("v", lambda e, pp=pp: e.tensor_mul(gtmp[pp], gtmp[pp], s_chunk[pp]))
                    DR("v")
                    EMD("v", lambda e, pp=pp: e.tensor_add(outsb, gtmp[pp], xtm[pp]), s_p5e)
                    W("y", s_p5e, s_p5e.n)
                    EM("y", lambda e, tb=tb: e.dma_start(
                        out=out_own[tb * 128:(tb + 1) * 128, :], in_=outsb), s_out, 16)
            W("y", s_out, 8 * 16)

        phases = [p1, p2_prep, lambda: p2_head(0), lambda: p2_head(1), p25, p4,
                  p3, p3b, p5]
        nph = len(phases) if PHASE_LIMIT is None else PHASE_LIMIT
        for ph in phases[:nph]:
            ph()

        if TAPS:
            bufs = {"qbuf": qbuf, "kbuf": kbuf, "vbuf": vbuf, "accb": accb,
                    "xn_own": xn_own, "hbuf": hbuf, "bias0": bias_all[0],
                    "bias1": bias_all[1], "nq2": nq2, "nk2": nk2,
                    "inv_bc": inv_bc, "wslab0": wslab[0],
                    "nmb": nmb, "varb": varb, "stdb": stdb, "rstdb": rstdb,
                    "negsum": negsum, "v1b": v1b, "m2nb": m2nb,
                    "xntm0": xn_tm[0], "xntm1": xn_tm[1],
                    "xnfm0": xn_fm[0], "xnfm1": xn_fm[1], "wqkv": wqkv_sb}
            s_tap = sem("s_tap")
            EMD("v", lambda e: e.engine_nop(), s_tap)
            W("y", s_tap, s_tap.n)
            for name in TAPS:
                ap = bufs[name]
                tdr = nc.dram_tensor(f"tap_{name}", list(ap.shape), ap.dtype,
                                     kind="ExternalOutput")
                W("y", s_acc, s_acc.n)
                W("y", s_fm, s_fm.n)
                W("y", s_h, s_h.n)
                W("y", s_bv, s_bv.n)
                EM("y", lambda e, tdr=tdr, ap=ap: e.dma_start(out=tdr[:, :], in_=ap),
                   s_tap, 16)
            W("y", s_tap, s_tap.n)

        block = es.enter_context(nc.Block())

        @block.tensor
        def _(e):
            for fn in prog.streams["t"]:
                fn(e)

        @block.vector
        def _(e):
            for fn in prog.streams["v"]:
                fn(e)

        @block.scalar
        def _(e):
            for fn in prog.streams["s"]:
                fn(e)

        @block.gpsimd
        def _(e):
            for fn in prog.streams["g"]:
                fn(e)

        @block.sync
        def _(e):
            for fn in prog.streams["y"]:
                fn(e)

    return nc


_NC_CACHE = {}


def _get_program():
    if "nc" not in _NC_CACHE:
        _NC_CACHE["nc"] = build_program()
    return _NC_CACHE["nc"]


def kernel(L0_attn_x, L0_attn_w_qkv, L0_attn_w_o, L0_ffn_w_gate_up, L0_ffn_w_proj):
    x = np.ascontiguousarray(np.asarray(L0_attn_x, dtype=np.float32).reshape(S, D))
    w_qkv = np.ascontiguousarray(np.asarray(L0_attn_w_qkv, dtype=np.float32))
    w_o = np.ascontiguousarray(np.asarray(L0_attn_w_o, dtype=np.float32))
    w_gu = np.ascontiguousarray(np.asarray(L0_ffn_w_gate_up, dtype=np.float32))
    w_p = np.ascontiguousarray(np.asarray(L0_ffn_w_proj, dtype=np.float32))

    ident = np.eye(128, dtype=np.float32).astype(ml_dtypes.bfloat16)
    ones = np.ones((128, 128), dtype=np.float32).astype(ml_dtypes.bfloat16)

    in_maps = []
    for c in range(NC):
        h0, h1 = 2 * c, 2 * c + 1
        cols = []
        for base in (0, 2048, 4096):
            cols.append(w_qkv[:, base + 128 * h0:base + 128 * (h0 + 1)])
            cols.append(w_qkv[:, base + 128 * h1:base + 128 * (h1 + 1)])
        in_maps.append({
            "x_full": x,
            "x_own": np.ascontiguousarray(x[OWN * c:OWN * (c + 1), :]),
            "w_qkvown": np.ascontiguousarray(np.concatenate(cols, axis=1)),
            "w_oown": np.ascontiguousarray(w_o[256 * c:256 * (c + 1), :]),
            "w_gu_shard": np.ascontiguousarray(w_gu[:, 256 * c:256 * (c + 1)]),
            "w_p_shard": np.ascontiguousarray(w_p[:, 1024 * c:1024 * (c + 1)]),
            "ident_bf": ident,
            "ones_bf": ones,
        })

    nc = _get_program()
    res = run_bass_kernel_spmd(nc, in_maps, core_ids=list(range(NC)))
    out = np.concatenate([res.results[c]["out_own"] for c in range(NC)], axis=0)
    return out.reshape(1, S, D).astype(np.float32)
